# version 31
# baseline (speedup 1.0000x reference)
import numpy as np
import ml_dtypes
from contextlib import ExitStack

import concourse.bass as bass
import concourse.mybir as mybir
from concourse.bass_utils import run_bass_kernel_spmd

F32 = mybir.dt.float32
BF16 = mybir.dt.bfloat16
AF = mybir.ActivationFunctionType
OP = mybir.AluOpType
AX = mybir.AxisListType

N_CORE = 8
SEQ = 2
N = 2048
D = 1024
NT = N // 128
NC = D // 128
HEADS = 16
DH = 64
WIN = 512
NWIN = N // WIN

ENGINES = ("sync", "tensor", "vector", "scalar", "gpsimd")
DMA_SEMS = {"ld", "st"}
SEMS = ("ld", "st", "pe", "dve", "act")
ENG_SEM = {"sync": "ld", "tensor": "pe", "vector": "dve", "scalar": "act", "gpsimd": "st"}


def _bucket_of(n):
    if n < 16:
        return n
    nf = np.float32(max(n, 1))
    v = np.float32(np.log(np.float32(nf / np.float32(16.0))))
    v = np.float32(v / np.float32(np.log(8.0)))
    v = np.float32(v * np.float32(16.0))
    val = 16 + int(v)
    return min(val, 31)


def _host_consts():
    ohd = np.zeros((32, 384), dtype=np.float32)
    for x in range(384):
        d = x - 127
        b = _bucket_of(d) if d >= 0 else 0
        ohd[b, x] = 1.0
    ident32 = np.eye(128, dtype=np.float32)
    identbf = np.eye(128, dtype=np.float32).astype(ml_dtypes.bfloat16)
    return ohd, ident32, identbf


class Ev:
    __slots__ = ("sem", "val", "force")

    def __init__(self, sem, val, force=False):
        self.sem = sem
        self.val = val
        self.force = force


class Res:

    __slots__ = ("w", "weng", "rs", "tiny")

    def __init__(self):
        self.w = None
        self.weng = None
        self.rs = {}
        self.tiny = False


class B:
    def __init__(self):
        self.ops = {e: [] for e in ENGINES}
        self.cnt = {s: 0 for s in SEMS}
        self.lastw = {e: {s: 0 for s in SEMS} for e in ENGINES}

    def emit(self, eng, builder, waits=(), mark=False):
        sem = ENG_SEM[eng]
        real = []
        for ev in waits:
            if ev is None:
                continue
            if ev.sem == sem and not ev.force:
                continue
            if ev.val > self.lastw[eng][ev.sem]:
                real.append((ev.sem, ev.val))
                self.lastw[eng][ev.sem] = ev.val
        ev = None
        if mark:
            amt = 16 if sem in DMA_SEMS else 1
            self.cnt[sem] += amt
            ev = Ev(sem, self.cnt[sem])
        self.ops[eng].append((real, builder, sem if mark else None))
        return ev

    def rd(self, eng, *ress):
        waits = []
        for r in ress:
            if r.w is not None and r.weng != eng:
                waits.append(r.w)
            elif r.w is None and r.weng is not None and r.weng != eng:
                raise RuntimeError("cross-engine read of unmarked write")
            elif r.weng == eng and r.tiny:
                if r.w is None:
                    raise RuntimeError("tiny write must be marked")
                waits.append(Ev(r.w.sem, r.w.val, force=True))
        return waits

    def wr(self, eng, *ress):
        waits = []
        for r in ress:
            if r.w is not None and r.weng != eng:
                waits.append(r.w)
            for reng, (rev, dangling) in r.rs.items():
                if reng != eng:
                    if dangling or rev is None:
                        raise RuntimeError("WAR on unmarked read")
                    waits.append(rev)
        return waits

    def did_rd(self, eng, ev, *ress):
        for r in ress:
            cur = r.rs.get(eng)
            if ev is None:
                if cur is None:
                    r.rs[eng] = [None, True]
                else:
                    cur[1] = True
            else:
                r.rs[eng] = [ev, False]

    def did_wr(self, eng, ev, *ress, tiny=False):
        for r in ress:
            r.w = ev
            r.weng = eng
            r.rs = {}
            r.tiny = tiny


def build_nc():
    nc = bass.Bass("TRN2", target_bir_lowering=False)
    b = B()

    x_hbm = nc.dram_tensor("x", [SEQ * N, D], F32, kind="ExternalInput").ap()
    wq_hbm = nc.dram_tensor("wq", [D, D], F32, kind="ExternalInput").ap()
    wkv_hbm = nc.dram_tensor("wkv", [D, 2 * DH], F32, kind="ExternalInput").ap()
    wout_hbm = nc.dram_tensor("wout", [D, D], F32, kind="ExternalInput").ap()
    bout_hbm = nc.dram_tensor("bout", [1, D], F32, kind="ExternalInput").ap()
    remb_hbm = nc.dram_tensor("remb", [32, HEADS], F32, kind="ExternalInput").ap()
    fw_hbm = nc.dram_tensor("fw", [D, D], F32, kind="ExternalInput").ap()
    fb_hbm = nc.dram_tensor("fb", [1, D], F32, kind="ExternalInput").ap()
    ohd_hbm = nc.dram_tensor("ohd", [32, 384], F32, kind="ExternalInput").ap()
    id32_hbm = nc.dram_tensor("id32", [128, 128], F32, kind="ExternalInput").ap()
    idbf_hbm = nc.dram_tensor("idbf", [128, 128], BF16, kind="ExternalInput").ap()
    y_hbm = nc.dram_tensor("y", [SEQ * N, D], F32, kind="ExternalOutput").ap()

    with ExitStack() as st_:
        E = st_.enter_context
        x_tok = E(nc.sbuf_tensor("x_tok", [128, NT * D], F32))
        xln2T = E(nc.sbuf_tensor("xln2T", [128, NC * N], BF16))
        wq_sb = E(nc.sbuf_tensor("wq_sb", [128, NC * D], BF16))
        wout_sb = E(nc.sbuf_tensor("wout_sb", [128, NC * D], BF16))
        wdT = E(nc.sbuf_tensor("wdT", [128, NC * D], BF16))
        qf = E(nc.sbuf_tensor("qf", [128, 8 * WIN], BF16))
        afeat = E(nc.sbuf_tensor("afeat", [128, 8 * WIN], BF16))
        eb = E(nc.sbuf_tensor("eb", [128, HEADS * 256], BF16))
        attn_t = E(nc.sbuf_tensor("attn_t", [128, 4 * WIN], BF16))
        kdup = E(nc.sbuf_tensor("kdup", [128, N], BF16))
        vaug = E(nc.sbuf_tensor("vaug", [128, NT * 128], BF16))
        wkv_sb = E(nc.sbuf_tensor("wkv_sb", [128, NC * 128], BF16))
        xmz = E(nc.sbuf_tensor("xmz", [128, D], F32))
        azdb = E(nc.sbuf_tensor("azdb", [128, D], BF16))
        dadd = E(nc.sbuf_tensor("dadd", [128, D], BF16))
        xdp = [E(nc.sbuf_tensor(f"xd{i}", [128, D], BF16)) for i in range(2)]
        _xdt0 = E(nc.sbuf_tensor("xdt0", [128, D], BF16))
        xdt = [_xdt0, _xdt0]
        rsb = E(nc.sbuf_tensor("rsb", [128, WIN], F32))
        vstg = rsb.ap().bitcast(BF16)
        cols = E(nc.sbuf_tensor("cols", [128, 400], F32))
        _xl_f32 = None
        remb_sb = E(nc.sbuf_tensor("remb_sb", [32, 16], F32))
        remb8 = E(nc.sbuf_tensor("remb8", [32, 16], F32))
        id32 = E(nc.sbuf_tensor("id32_sb", [128, 128], F32))
        idbf = E(nc.sbuf_tensor("idbf_sb", [128, 128], BF16))
        onesb = E(nc.sbuf_tensor("onesb", [1, 128], BF16))
        onesf = E(nc.sbuf_tensor("onesf", [1, 128], F32))
        bout_sb = E(nc.sbuf_tensor("bout_sb", [1, D], BF16))
        hbias_sb = E(nc.sbuf_tensor("hbias_sb", [1, D], BF16))
        negc_col = E(nc.sbuf_tensor("negc_col", [128, 1], F32))
        ccol = E(nc.sbuf_tensor("ccol", [128, 3], F32))
        scal = E(nc.sbuf_tensor("scal", [1, 16], F32))
        bvstage = E(nc.sbuf_tensor("bvstage", [128, 16], F32))
        pS4 = E(nc.psum_tensor("pS4", [128, 2048], F32))
        pS = [pS4.ap()[:, i * 512:(i + 1) * 512] for i in range(4)]
        pX = [pS[2], pS[3]]
        _pS4bf = pS4.ap().bitcast(BF16)
        pT = [_pS4bf[:, 0:1024], _pS4bf[:, 1024:2048]]
        pava = E(nc.psum_tensor("pava", [128, 1024], F32))
        pavb = E(nc.psum_tensor("pavb", [128, 1024], F32))
        pav = [pava.ap()[:, 0:512], pava.ap()[:, 512:1024],
               pavb.ap()[:, 0:512], pavb.ap()[:, 512:1024]]
        sems = {s: E(nc.semaphore(s)) for s in SEMS}
        block = E(nc.Block())

        qf_f32 = qf.ap().bitcast(F32)
        _xl32 = xln2T.ap().bitcast(F32)
        ohd_sb = _xl32[0:32, 0:384]
        bvT = _xl32[0:16, 384:768]
        expbv = xln2T.ap()[0:16, 1536:1920]

        R = lambda: Res()
        r_x = [R() for _ in range(NT)]
        r_xln2T = R()
        r_wq, r_wout, r_wdT = R(), R(), R()
        r_qf, r_afeat, r_eb = R(), R(), R()
        r_attn = [R() for _ in range(4)]
        r_kdup = [R() for _ in range(4)]
        r_vaug = [R() for _ in range(4)]
        r_wkv, r_vstg = R(), R()
        r_xmz, r_azdb, r_dadd = R(), R(), R()
        r_xdp = [R(), R()]
        _rxdt = R()
        r_xdt = [_rxdt, _rxdt]
        r_rsb = R()
        r_cols = [R(), R()]
        r_negc = R()
        r_scal = R()
        r_pS = [R(), R(), R(), R()]
        r_pav = [R() for _ in range(4)]
        r_pT = [r_pS[0], r_pS[1]]
        r_pX = [r_pS[2], r_pS[3]]
        r_stage = [R(), R()]
        r_bv = R()
        r_expbv = R()
        r_id32, r_idbf, r_ohd, r_remb, r_remb8 = R(), R(), R(), R(), R()
        r_onesb, r_onesf, r_bout, r_hbias, r_nc31 = R(), R(), R(), R(), R()
        r_ccol = R()

        def dma(dst, src, waits=(), rd_res=(), wr_res=(), eng="sync"):
            waits = list(waits) + b.rd(eng, *rd_res) + b.wr(eng, *wr_res)
            ev = b.emit(eng, lambda e, d=dst, s=src: e.dma_start(out=d, in_=s),
                        waits, mark=True)
            b.did_rd(eng, ev, *rd_res)
            b.did_wr(eng, ev, *wr_res)
            return ev

        def op(eng, builder, rd_res=(), wr_res=(), waits=(), mark=False, tiny=False):
            if tiny:
                mark = True
            waits = list(waits) + b.rd(eng, *rd_res) + b.wr(eng, *wr_res)
            ev = b.emit(eng, builder, waits, mark=mark)
            b.did_rd(eng, ev, *rd_res)
            b.did_wr(eng, ev, *wr_res, tiny=tiny)
            return ev

        dma(id32[:], id32_hbm, wr_res=(r_id32,))
        dma(idbf[:], idbf_hbm, wr_res=(r_idbf,))
        dma(ohd_sb[:], ohd_hbm, wr_res=(r_ohd,))
        dma(remb_sb[:], remb_hbm, wr_res=(r_remb,))

        op("vector", lambda e: e.memset(onesb[:], 1.0), wr_res=(r_onesb,), mark=True)
        op("vector", lambda e: e.memset(onesf[:], 1.0), wr_res=(r_onesf,), mark=True)
        op("vector", lambda e: e.memset(ccol[:, 0:1], 1e-5), wr_res=(r_ccol,), mark=True)
        op("vector", lambda e: e.memset(ccol[:, 1:2], -0.5), wr_res=(r_ccol,), mark=True)
        op("vector", lambda e: e.memset(ccol[:, 2:3], -1.0), wr_res=(r_ccol,), mark=True)
        va_ones = vaug[:].rearrange("p (t c) -> p t c", c=128)[:, :, 64:128]
        ev_vaones = op("vector", lambda e: e.memset(va_ones, 281474976710656.0),
                       wr_res=tuple(r_vaug), mark=True)

        def load_cast(dst_sb, dst_res, hbm, ncols, scale, out_bf=True, quant=False):
            for kc in range(NC):
                stg = qf_f32[:, (kc % 2) * 1024:(kc % 2) * 1024 + ncols]
                rstg = r_stage[kc % 2]
                dma(stg, hbm[kc * 128:(kc + 1) * 128, :], wr_res=(rstg,))
                dst = dst_sb[:, kc * ncols:(kc + 1) * ncols]
                if scale == 1.0:
                    ev = op("vector", lambda e, d=dst, s=stg: e.tensor_copy(d, s),
                            rd_res=(rstg,), wr_res=(dst_res,), mark=True)
                else:
                    ev = op("vector",
                            lambda e, d=dst, s=stg, sc=scale: e.tensor_scalar(
                                d, s, sc, None, OP.mult),
                            rd_res=(rstg,), wr_res=(dst_res,), mark=True)
            return ev

        stgrow = qf_f32[0:1, 0:D]
        dma(stgrow, bout_hbm, wr_res=(r_stage[0],))
        op("vector", lambda e: e.tensor_copy(bout_sb[:], stgrow),
           rd_res=(r_stage[0],), wr_res=(r_bout,), mark=True)
        dma(stgrow, fb_hbm, wr_res=(r_stage[0],))
        op("vector", lambda e: e.tensor_scalar(hbias_sb[:], stgrow, 0.5, None, OP.mult),
           rd_res=(r_stage[0],), wr_res=(r_hbias,), mark=True)

        def quant_tile(src_f32_ap, src_res, nmean_col, rstd75_col, rstd8_col,
                       out_bf_ap, out_res, colpar):
            rc = r_cols[colpar]
            c_bmax = cols[:, colpar * 128 + 0:colpar * 128 + 32]
            c_xs = cols[:, colpar * 128 + 32:colpar * 128 + 64]
            c_rxs = cols[:, colpar * 128 + 64:colpar * 128 + 96]
            c_rr8 = cols[:, colpar * 128 + 96:colpar * 128 + 128]
            if nmean_col is not None:
                op("scalar", lambda e: e.activation(
                    xmz[:], src_f32_ap, AF.Identity, bias=nmean_col, scale=1.0),
                   rd_res=(src_res, r_ln), wr_res=(r_xmz,), mark=True)
                zsrc, zres = xmz[:], r_xmz
            else:
                zsrc, zres = src_f32_ap, src_res
            op("vector", lambda e: e.tensor_reduce(
                c_bmax, zsrc.rearrange("p (b k) -> p b k", k=32), AX.X, OP.max,
                apply_absolute_value=True),
               rd_res=(zres,), wr_res=(rc,), tiny=True)
            if nmean_col is not None:
                op("vector", lambda e: e.tensor_scalar(
                    c_xs, c_bmax, rstd75_col, 1e-30, OP.mult, OP.max),
                   rd_res=(rc, r_ln), wr_res=(rc,), tiny=True)
                op("vector", lambda e: e.reciprocal(c_rxs, c_xs),
                   rd_res=(rc,), wr_res=(rc,), tiny=True)
                op("vector", lambda e: e.tensor_scalar(
                    c_rr8, c_rxs, rstd8_col, None, OP.mult),
                   rd_res=(rc, r_ln), wr_res=(rc,), tiny=True)
                final_scale = 1.0 / 16.0
            else:
                op("vector", lambda e: e.tensor_scalar(
                    c_xs, c_bmax, 1.0 / 7.5, 1e-30, OP.mult, OP.max),
                   rd_res=(rc,), wr_res=(rc,), tiny=True)
                op("vector", lambda e: e.reciprocal(c_rxs, c_xs),
                   rd_res=(rc,), wr_res=(rc,), tiny=True)
                op("vector", lambda e: e.tensor_scalar(
                    c_rr8, c_rxs, 8.0, None, OP.mult),
                   rd_res=(rc,), wr_res=(rc,), tiny=True)
                final_scale = 1.0 / 8.0
            c_xsf = c_rxs
            op("vector", lambda e: e.tensor_scalar(
                c_xsf, c_xs, final_scale, None, OP.mult),
               rd_res=(rc,), wr_res=(rc,), tiny=True)
            bc = lambda colap: colap.unsqueeze(2).broadcast_to([128, 32, 1]).broadcast_to([128, 32, 32])
            z3 = zsrc.rearrange("p (b k) -> p b k", k=32)
            ev_z = op("vector", lambda e: e.tensor_tensor(z3, z3, bc(c_rr8), OP.mult),
                      rd_res=(rc,), wr_res=(zres,), mark=True)
            ev_az = op("vector", lambda e: e.scalar_tensor_tensor(
                azdb[:], zsrc, -1.0, zsrc, OP.mult, OP.max),
                       rd_res=(zres,), wr_res=(r_azdb,), mark=True)
            op("vector", lambda e: e.tensor_scalar(
                dadd[:], azdb[:], 16.0, 192.0, OP.is_ge, OP.mult),
               rd_res=(r_azdb,), wr_res=(r_dadd,), mark=True)
            op("vector", lambda e: e.tensor_scalar(
                azdb[:], azdb[:], 32.0, 384.0, OP.is_ge, OP.mult),
               rd_res=(r_azdb,), wr_res=(r_azdb,), mark=True)
            op("vector", lambda e: e.tensor_tensor(dadd[:], dadd[:], azdb[:], OP.add),
               rd_res=(r_azdb,), wr_res=(r_dadd,), mark=True)
            op("vector", lambda e, u=out_bf_ap: e.scalar_tensor_tensor(
                u, zsrc, 192.0, dadd[:], OP.add, OP.add),
               rd_res=(zres, r_dadd), wr_res=(out_res,), mark=True)
            op("vector", lambda e, u=out_bf_ap: e.scalar_tensor_tensor(
                azdb[:], u, -192.0, dadd[:], OP.add, OP.subtract),
               rd_res=(out_res, r_dadd), wr_res=(r_azdb,), mark=True)
            ev = op("vector", lambda e: e.tensor_tensor(
                out_bf_ap.rearrange("p (b k) -> p b k", k=32),
                azdb[:].rearrange("p (b k) -> p b k", k=32), bc(c_xsf), OP.mult),
                rd_res=(r_azdb, rc), wr_res=(out_res,), mark=True)
            return ev

        for jc in range(NC):
            stg = qf_f32[:, (jc % 2) * 1024:(jc % 2) * 1024 + D]
            rstg = r_stage[jc % 2]
            dma(stg, fw_hbm[jc * 128:(jc + 1) * 128, :], wr_res=(rstg,))
            quant_tile(stg, rstg, None, None, None, xdp[jc % 2][:],
                       r_xdp[jc % 2], jc % 2)
            pt, rpt = pT[jc % 2], r_pT[jc % 2]
            xdl, rxdl = xdp[jc % 2], r_xdp[jc % 2]
            for kc in range(NC):
                op("tensor", lambda e, k=kc, p=pt, xx=xdl: e.transpose(
                    p[:, k * 128:(k + 1) * 128],
                    xx[:, k * 128:(k + 1) * 128], idbf[:]),
                   rd_res=(rxdl, r_idbf), wr_res=(rpt,),
                   mark=(kc == NC - 1))
            dst = wdT[:].rearrange("p (k j) -> p k j", j=D)[:, :, jc * 128:(jc + 1) * 128]
            src = pt.rearrange("p (k t) -> p k t", t=128)
            op("vector", lambda e, d=dst, s=src: e.tensor_copy(d, s),
               rd_res=(rpt,), wr_res=(r_wdT,), mark=True)

        for tau in range(NT):
            dma(x_tok[:, tau * D:(tau + 1) * D],
                x_hbm[tau * 128:(tau + 1) * 128, :], wr_res=(r_x[tau],))

        r_ebpre = [R(), R()]

        def load_cast_chunk(dst_sb, dst_res, hbm, kc, scale):
            stg = qf_f32[:, (kc % 2) * 1024:(kc % 2) * 1024 + D]
            rstg = r_stage[kc % 2]
            dma(stg, hbm[kc * 128:(kc + 1) * 128, :], wr_res=(rstg,))
            dst = dst_sb[:, kc * D:(kc + 1) * D]
            if scale == 1.0:
                op("vector", lambda e, d=dst, s2=stg: e.tensor_copy(d, s2),
                   rd_res=(rstg,), wr_res=(dst_res,), mark=True)
            else:
                op("vector", lambda e, d=dst, s2=stg, sc=scale: e.tensor_scalar(
                    d, s2, sc, None, OP.mult),
                   rd_res=(rstg,), wr_res=(dst_res,), mark=True)

        def bias_round(ch):
            op("tensor", lambda e, c=ch: e.matmul(
                pS[2][:, 0:16], ohd_sb[:, c * 128:(c + 1) * 128], remb8[:],
                start=True, stop=True),
               rd_res=(r_ohd, r_remb8), wr_res=(r_pS[2],), mark=True)
            op("vector", lambda e: e.tensor_copy(bvstage[:], pS[2][:, 0:16]),
               rd_res=(r_pS[2],), wr_res=(r_bv,), mark=True)
            op("tensor", lambda e: e.transpose(
                pS[3][0:16, 0:128], bvstage[:], id32[:]),
               rd_res=(r_bv, r_id32), wr_res=(r_pS[3],), mark=True)
            op("vector", lambda e, c=ch: e.tensor_copy(
                bvT[:, c * 128:(c + 1) * 128], pS[3][0:16, 0:128]),
               rd_res=(r_pS[3],), wr_res=(r_bv,), mark=True)

        def emit_eb_head(h):
            par = h % 2
            ebt = xln2T.ap().tensor
            eboff = 1536
            ebpre = afeat[:, par * 256:par * 256 + 256]
            for g in range(4):
                rowlen = xln2T.ap().shape[1]
                src = bass.AP(ebt, h * rowlen + eboff + 32 * g,
                              [[rowlen, 1], [1, 32], [1, 256]])
                dma(ebpre[32 * (3 - g):32 * (4 - g), :], src,
                    rd_res=(r_expbv,), wr_res=(r_ebpre[par],))
            op("vector", lambda e, hh=h, ep=ebpre: e.stream_shuffle(
                eb[:, hh * 256:(hh + 1) * 256], ep, list(range(31, -1, -1))),
               rd_res=(r_ebpre[par], r_afeat), wr_res=(r_eb,), mark=True)

        def setup_hook(i):
            if i == 0:
                op("vector", lambda e: e.tensor_scalar(
                    remb8[:], remb_sb[:], 8.0, None, OP.mult),
                   rd_res=(r_remb,), wr_res=(r_remb8,), mark=True)
                bias_round(0)
            elif i in (1, 2):
                bias_round(i)
            elif i == 3:
                op("vector", lambda e: e.tensor_scalar(
                    cols[0:16, 386:387], bvT[:, 327:328], -1.0, None, OP.mult),
                   rd_res=(r_bv,), wr_res=(r_nc31,), mark=True)
                op("scalar", lambda e: e.activation(
                    expbv[:], bvT[:], AF.Exp, bias=cols[0:16, 386:387], scale=1.0),
                   rd_res=(r_bv, r_nc31), wr_res=(r_expbv,), mark=True)
                op("vector", lambda e: e.memset(expbv[:, 0:127], 0.0),
                   wr_res=(r_expbv,), mark=True)
            elif i <= 11:
                emit_eb_head(2 * (i - 4))
                emit_eb_head(2 * (i - 4) + 1)
            else:
                for k in range(4):
                    g = 4 * (i - 12) + k
                    if g < 8:
                        load_cast_chunk(wq_sb, r_wq, wq_hbm, g, 0.125)
                    else:
                        load_cast_chunk(wout_sb, r_wout, wout_hbm, g - 8, 1.0)
                for k2 in range(2):
                    kc = 2 * (i - 12) + k2
                    stg = rsb[:, (kc % 2) * 128:(kc % 2) * 128 + 128]
                    dma(stg, wkv_hbm[kc * 128:(kc + 1) * 128, :],
                        wr_res=(r_vstg, r_rsb))
                    op("vector", lambda e, d=wkv_sb[:, kc * 128:(kc + 1) * 128],
                       s2=stg: e.tensor_copy(d, s2),
                       rd_res=(r_vstg,), wr_res=(r_wkv,), mark=True)

        CB = 256
        A_SUM, A_SQ, A_MEAN, A_VAR, A_RSTD, A_R75, A_R8, A_NMR = range(8)
        r_ln = R()

        def c_a(area, t0, n=1):
            return cols[:, CB + area * 16 + t0:CB + area * 16 + t0 + n]

        def ln_stats_batch(taus, need_nmr=False):
            t0, n = taus[0], len(taus)
            for tau in taus:
                xap = x_tok[:, tau * D:(tau + 1) * D]
                xres = r_x[tau]
                op("scalar", lambda e, xa=xap, t=tau: e.activation(
                    xmz[:], xa, AF.Identity, accum_out=c_a(A_SUM, t)),
                   rd_res=(xres,), wr_res=(r_xmz, r_ln), mark=True)
                op("scalar", lambda e, xa=xap, t=tau: e.activation(
                    xmz[:], xa, AF.Square, accum_out=c_a(A_SQ, t)),
                   rd_res=(xres,), wr_res=(r_xmz, r_ln), mark=True)
            op("vector", lambda e, a=t0, k=n: e.tensor_scalar(
                c_a(A_MEAN, a, k), c_a(A_SUM, a, k), 1.0 / D, None, OP.mult),
               rd_res=(r_ln,), wr_res=(r_ln,), tiny=True)
            op("vector", lambda e, a=t0, k=n: e.tensor_scalar(
                c_a(A_SUM, a, k), c_a(A_MEAN, a, k), -1.0, None, OP.mult),
               rd_res=(r_ln,), wr_res=(r_ln,), tiny=True)
            op("vector", lambda e, a=t0, k=n: e.tensor_scalar(
                c_a(A_VAR, a, k), c_a(A_SQ, a, k), 1.0 / D, None, OP.mult),
               rd_res=(r_ln,), wr_res=(r_ln,), tiny=True)
            op("vector", lambda e, a=t0, k=n: e.scalar_tensor_tensor(
                c_a(A_SQ, a, k), c_a(A_MEAN, a, k), 1.0, c_a(A_MEAN, a, k),
                OP.mult, OP.mult),
               rd_res=(r_ln,), wr_res=(r_ln,), tiny=True)
            ev = op("vector", lambda e, a=t0, k=n: e.tensor_tensor(
                c_a(A_VAR, a, k), c_a(A_VAR, a, k), c_a(A_SQ, a, k), OP.subtract),
                rd_res=(r_ln,), wr_res=(r_ln,), tiny=True)
            op("scalar", lambda e, a=t0, k=n: e.activation(
                c_a(A_RSTD, a, k), c_a(A_VAR, a, k), AF.Ln, bias=ccol[:, 0:1],
                scale=1.0),
               rd_res=(r_ln, r_ccol), wr_res=(r_ln,), waits=[ev], mark=True)
            ev2 = op("scalar", lambda e, a=t0, k=n: e.activation(
                c_a(A_RSTD, a, k), c_a(A_RSTD, a, k), AF.Exp, bias=0.0,
                scale=ccol[:, 1:2]),
                rd_res=(r_ln, r_ccol), wr_res=(r_ln,), mark=True)
            op("vector", lambda e, a=t0, k=n: e.tensor_scalar(
                c_a(A_R75, a, k), c_a(A_RSTD, a, k), 1.0 / 7.5, None, OP.mult),
               rd_res=(r_ln,), wr_res=(r_ln,), waits=[ev2], tiny=True)
            op("vector", lambda e, a=t0, k=n: e.tensor_scalar(
                c_a(A_R8, a, k), c_a(A_RSTD, a, k), 8.0, None, OP.mult),
               rd_res=(r_ln,), wr_res=(r_ln,), tiny=True)
            if need_nmr:
                op("vector", lambda e, a=t0, k=n: e.scalar_tensor_tensor(
                    c_a(A_NMR, a, k), c_a(A_MEAN, a, k), -1.0, c_a(A_RSTD, a, k),
                    OP.mult, OP.mult),
                   rd_res=(r_ln,), wr_res=(r_ln,), tiny=True)

        def ffn_tile_mm(s, tau):
            par = tau % 2
            xap = x_tok[:, tau * D:(tau + 1) * D]
            xres = r_x[tau]
            quant_tile(xap, xres, c_a(A_SUM, tau), c_a(A_R75, tau),
                       c_a(A_R8, tau), xdp[par][:], r_xdp[par], par)
            pt, rpt = pT[par], r_pT[par]
            for kc in range(NC):
                op("tensor", lambda e, k=kc, p=pt, xx=xdp[par]: e.transpose(
                    p[:, k * 128:(k + 1) * 128], xx[:, k * 128:(k + 1) * 128],
                    idbf[:]),
                   rd_res=(r_xdp[par], r_idbf), wr_res=(rpt,), mark=(kc == NC - 1))
            xt, rxt = xdt[par], r_xdt[par]
            op("scalar", lambda e, p=pt, d=xt: e.copy(d[:], p),
               rd_res=(rpt,), wr_res=(rxt,), mark=True)
            bt = pava if par == 0 else pavb
            rbt = [r_pav[0], r_pav[1]] if par == 0 else [r_pav[2], r_pav[3]]
            banks = [(bt.ap()[:, 0:512], rbt[0]), (bt.ap()[:, 512:1024], rbt[1])]
            for kc in range(NC):
                for half in range(2):
                    ps, rps = banks[half]
                    op("tensor", lambda e, k=kc, h=half, p=ps, xx=xt: e.matmul(
                        p, xx[:, k * 128:(k + 1) * 128],
                        wdT[:, k * D + h * 512:k * D + (h + 1) * 512],
                        start=(k == 0), stop=False),
                       rd_res=(rxt, r_wdT), wr_res=(rps,),
                       mark=(kc == NC - 1))
            for half in range(2):
                ps, rps = banks[half]
                op("tensor", lambda e, h=half, p=ps: e.matmul(
                    p, onesb[0:1, :], hbias_sb[0:1, h * 512:(h + 1) * 512],
                    start=False, stop=True),
                   rd_res=(r_onesb, r_hbias), wr_res=(rps,), mark=True)

            def epilogue():
                op("vector", lambda e, p=bt, xa=xap: e.tensor_tensor(
                    xa, xa, p.ap(), OP.add),
                   rd_res=(rbt[0], rbt[1]), wr_res=(xres,), mark=True)
            return epilogue

        def ffn_run(s, taus, hook=None):
            pend = None
            for i, tau in enumerate(taus):
                nxt = ffn_tile_mm(s, tau)
                if pend is not None:
                    pend()
                pend = nxt
                if hook is not None:
                    hook(i)
            pend()

        def ln2_tile(s, tau):
            par = tau % 2
            xap = x_tok[:, tau * D:(tau + 1) * D]
            xres = r_x[tau]
            xl, rxl = xdp[par], r_xdp[par]
            op("scalar", lambda e, t=tau, d=xl: e.activation(
                d[:], xap, AF.Identity, bias=c_a(A_NMR, t), scale=c_a(A_RSTD, t)),
               rd_res=(xres, r_ln), wr_res=(rxl,), mark=True)
            pt, rpt = pT[par], r_pT[par]
            for kc in range(NC):
                op("tensor", lambda e, k=kc, p=pt, xx=xl: e.transpose(
                    p[:, k * 128:(k + 1) * 128], xx[:, k * 128:(k + 1) * 128], idbf[:]),
                   rd_res=(rxl, r_idbf), wr_res=(rpt,), mark=(kc == NC - 1))
            dst = xln2T[:].rearrange("p (k t) -> p k t", t=N)[:, :, tau * 128:(tau + 1) * 128]
            src = pt.rearrange("p (k t) -> p k t", t=128)
            op("vector", lambda e, d=dst, sr=src: e.tensor_copy(d, sr),
               rd_res=(rpt,), wr_res=(r_xln2T, r_ohd, r_bv, r_expbv), mark=True)

        def kv_chunk(s, tc):
            ps, rps = pav[tc % 2], r_pav[tc % 2]
            for kc in range(NC):
                op("tensor", lambda e, k=kc, t=tc, p=ps: e.matmul(
                    p, wkv_sb[:, k * 128:(k + 1) * 128],
                    xln2T[:, k * N + t * 512:k * N + (t + 1) * 512],
                    start=(k == 0), stop=(k == NC - 1)),
                   rd_res=(r_wkv, r_xln2T), wr_res=(rps,),
                   mark=(kc == NC - 1))
            op("scalar", lambda e, t=tc, p=ps: e.copy(
                kdup[0:64, t * 512:(t + 1) * 512], p[0:64, :]),
               rd_res=(rps,), wr_res=(r_kdup[tc],), mark=True)
            op("scalar", lambda e, t=tc, p=ps: e.copy(
                kdup[64:128, t * 512:(t + 1) * 512], p[0:64, :]),
               rd_res=(rps,), wr_res=(r_kdup[tc],), mark=True)
            op("scalar", lambda e, p=ps: e.copy(
                vstg[0:64, 0:512], p[64:128, :]),
               rd_res=(rps,), wr_res=(r_vstg,), mark=True)
            for jl in range(4):
                op("tensor", lambda e, j=jl: e.transpose(
                    pT[0][:, j * 64:(j + 1) * 64], vstg[0:64, j * 128:(j + 1) * 128],
                    idbf[0:64, 0:64]),
                   rd_res=(r_vstg, r_idbf), wr_res=(r_pT[0],), mark=(jl == 3))
            dst = vaug[:].rearrange("p (t c) -> p t c", c=128)[:, tc * 4:tc * 4 + 4, 0:64]
            src = pT[0][:, 0:256].rearrange("p (t c) -> p t c", c=64)
            op("vector", lambda e, d=dst, sr=src: e.tensor_copy(d, sr),
               rd_res=(r_pT[0],), wr_res=(r_vaug[tc],), mark=True)
            op("vector", lambda e, t=tc: e.tensor_reduce(
                cols[0:64, 384:385], kdup[0:64, t * 512:(t + 1) * 512], AX.X,
                OP.max, apply_absolute_value=True),
               rd_res=(r_kdup[tc],), wr_res=(r_scal,), tiny=True)
            op("tensor", lambda e: e.transpose(
                pS[0][0:1, 0:64], cols[0:64, 384:385], id32[0:64, 0:64]),
               rd_res=(r_scal, r_id32), wr_res=(r_pS[0],), mark=True)
            if tc == 0:
                op("vector", lambda e: e.tensor_reduce(
                    scal[0:1, 0:1], pS[0][0:1, 0:64], AX.X, OP.max),
                   rd_res=(r_pS[0],), wr_res=(r_scal,), tiny=True)
            else:
                op("vector", lambda e: e.tensor_reduce(
                    scal[0:1, 3:4], pS[0][0:1, 0:64], AX.X, OP.max),
                   rd_res=(r_pS[0],), wr_res=(r_scal,), tiny=True)
                op("vector", lambda e: e.tensor_tensor(
                    scal[0:1, 0:1], scal[0:1, 0:1], scal[0:1, 3:4], OP.max),
                   rd_res=(r_scal,), wr_res=(r_scal,), tiny=True)

        def attn_window(s, ic):
            for m in range(NC):
                ps, rps = (pS[2], r_pS[2]) if m % 2 == 0 else (pS[3], r_pS[3])
                for kc in range(NC):
                    op("tensor", lambda e, k=kc, mm=m, p=ps: e.matmul(
                        p, wq_sb[:, k * D + mm * 128:k * D + mm * 128 + 128],
                        xln2T[:, k * N + ic * 512:k * N + (ic + 1) * 512],
                        start=(k == 0), stop=(k == NC - 1)),
                       rd_res=(r_wq, r_xln2T), wr_res=(rps,), mark=(kc == NC - 1))
                op("vector", lambda e, mm=m, p=ps: e.tensor_copy(
                    qf[:, mm * 512:(mm + 1) * 512], p[:]),
                   rd_res=(rps,), wr_res=(r_qf,), mark=True)
            op("vector", lambda e: e.tensor_reduce(
                cols[:, 385:386], qf[:], AX.X, OP.max, apply_absolute_value=True),
               rd_res=(r_qf,), wr_res=(r_scal,), tiny=True)
            op("tensor", lambda e: e.transpose(
                pS[0][0:1, 0:128], cols[:, 385:386], id32[:]),
               rd_res=(r_scal, r_id32), wr_res=(r_pS[0],), mark=True)
            op("vector", lambda e: e.tensor_reduce(
                scal[0:1, 1:2], pS[0][0:1, 0:128], AX.X, OP.max),
               rd_res=(r_pS[0],), wr_res=(r_scal,), tiny=True)
            op("vector", lambda e: e.scalar_tensor_tensor(
                scal[0:1, 2:3], scal[0:1, 1:2], -64.0, scal[0:1, 0:1],
                OP.mult, OP.mult),
               rd_res=(r_scal,), wr_res=(r_scal,), tiny=True)
            op("tensor", lambda e: e.matmul(
                pS[0][:, 0:1], onesf[0:1, :], scal[0:1, 2:3],
                start=True, stop=True),
               rd_res=(r_scal, r_onesf), wr_res=(r_pS[0],), mark=True)
            op("vector", lambda e: e.tensor_copy(
                negc_col[:], pS[0][:, 0:1]),
               rd_res=(r_pS[0],), wr_res=(r_negc,), tiny=True)

            jts = 4 * ic + 4
            for hg in range(4):
                sb_of = [[pS[0], pS[1]], [pS[2], pS[3]]]
                rb_of = [[r_pS[0], r_pS[1]], [r_pS[2], r_pS[3]]]

                def qk_emit(jt, p2):
                    il = max(0, jt * 128 - ic * 512)
                    m = 2 * hg + p2
                    sbank, rbank = sb_of[p2], rb_of[p2]
                    op("tensor", lambda e, j=jt, mm=m, ill=il, sb=sbank: e.matmul(
                        sb[0][:, ill:512], kdup[0:64, j * 128:(j + 1) * 128],
                        qf[0:64, mm * 512 + ill:(mm + 1) * 512],
                        start=True, stop=True, tile_position=(0, 0)),
                       rd_res=(r_kdup[jt // 4], r_qf), wr_res=(rbank[0],), mark=True)
                    op("tensor", lambda e, j=jt, mm=m, ill=il, sb=sbank: e.matmul(
                        sb[1][:, ill:512], kdup[64:128, j * 128:(j + 1) * 128],
                        qf[64:128, mm * 512 + ill:(mm + 1) * 512],
                        start=True, stop=True, tile_position=(64, 0)),
                       rd_res=(r_kdup[jt // 4], r_qf), wr_res=(rbank[1],), mark=True)

                qk_emit(0, 0)
                qk_emit(0, 1)
                ps4r = pS4.ap().rearrange("p (h w) -> p h w", h=4)
                at4r = attn_t[:].rearrange("p (h w) -> p h w", h=4)
                for jt in range(jts):
                    il = max(0, jt * 128 - ic * 512)
                    ia = max(ic * 512 + il, jt * 128) - jt * 128
                    ib = min(256, ic * 512 + 512 - jt * 128)
                    for p2 in range(2):
                        h0 = 2 * p2
                        ps3 = ps4r[:, h0:h0 + 2, il:512]
                        at3 = at4r[:, h0:h0 + 2, il:512]
                        op("scalar", lambda e, d=at3, sr=ps3: e.activation(
                            d, sr, AF.Exp, bias=negc_col[:], scale=1.0),
                           rd_res=(r_pS[h0], r_pS[h0 + 1], r_negc),
                           wr_res=(r_attn[h0], r_attn[h0 + 1]),
                           mark=True)
                        if ia < ib:
                            la = jt * 128 + ia - ic * 512
                            lb = jt * 128 + ib - ic * 512
                            for hs2_ in (h0, h0 + 1):
                                hh2 = 4 * hg + hs2_
                                op("vector", lambda e, h3=hh2, h4=hs2_, a=ia, bb=ib,
                                   l1=la, l2=lb: e.tensor_tensor(
                                       attn_t[:, h4 * 512 + l1:h4 * 512 + l2],
                                       attn_t[:, h4 * 512 + l1:h4 * 512 + l2],
                                       eb[:, h3 * 256 + a:h3 * 256 + bb], OP.mult),
                                   rd_res=(r_eb,), wr_res=(r_attn[hs2_],), mark=True)
                        if jt + 1 < jts:
                            qk_emit(jt + 1, p2)
                        for hs in (h0, h0 + 1):
                            op("tensor", lambda e, j=jt, hs2=hs, ill=il: e.matmul(
                                pav[hs2][:, ill:512], vaug[:, j * 128:(j + 1) * 128],
                                attn_t[:, hs2 * 512 + ill:(hs2 + 1) * 512],
                                start=(j == 0), stop=(j == jts - 1)),
                               rd_res=(r_vaug[jt // 4], r_attn[hs]),
                               wr_res=(r_pav[hs],), mark=True)
                for hs in range(4):
                    hh = 4 * hg + hs
                    m = hh // 2
                    rows = (hh % 2) * 64
                    op("scalar", lambda e, h2=hs: e.activation(
                        rsb[64:128, :], pav[h2][64:128, :], AF.Ln),
                       rd_res=(r_pav[hs],), wr_res=(r_rsb,), mark=True)
                    op("scalar", lambda e, h2=hs: e.activation(
                        pav[h2][64:128, :], rsb[64:128, :], AF.Exp, bias=0.0,
                        scale=ccol[64:128, 1:2]),
                       rd_res=(r_rsb, r_ccol), wr_res=(r_pav[hs], r_rsb), mark=True)
                    op("scalar", lambda e, h2=hs: e.activation(
                        rsb[64:128, :], pav[h2][64:128, :], AF.Square),
                       rd_res=(r_pav[hs],), wr_res=(r_rsb,), mark=True)
                    op("vector", lambda e, h2=hs, mm=m, rw=rows: e.scalar_tensor_tensor(
                        afeat[rw:rw + 64, mm * 512:(mm + 1) * 512],
                        pav[h2][0:64, :], 281474976710656.0, rsb[64:128, :],
                        OP.mult, OP.mult),
                       rd_res=(r_pav[hs], r_rsb), wr_res=(r_afeat,), mark=True)
            wpend = None
            for tl in range(4):
                tau = ic * 4 + tl
                xres = r_x[tau]
                xap = x_tok[:, tau * D:(tau + 1) * D]
                bt = pava if tl % 2 == 0 else pavb
                rbt = [r_pav[0], r_pav[1]] if tl % 2 == 0 else [r_pav[2], r_pav[3]]
                banks = [(bt.ap()[:, 0:512], rbt[0]), (bt.ap()[:, 512:1024], rbt[1])]
                for kc in range(NC):
                    for half in range(2):
                        ps, rps = banks[half]
                        op("tensor", lambda e, k=kc, t=tl, h=half, p=ps: e.matmul(
                            p, afeat[:, k * 512 + t * 128:k * 512 + (t + 1) * 128],
                            wout_sb[:, k * D + h * 512:k * D + (h + 1) * 512],
                            start=(k == 0), stop=False),
                           rd_res=(r_afeat, r_wout), wr_res=(rps,),
                           mark=(kc == NC - 1))
                for half in range(2):
                    ps, rps = banks[half]
                    op("tensor", lambda e, h=half, p=ps: e.matmul(
                        p, onesb[0:1, :], bout_sb[0:1, h * 512:(h + 1) * 512],
                        start=False, stop=True),
                       rd_res=(r_onesb, r_bout), wr_res=(rps,), mark=True)

                def wepi(bt=bt, rbt=rbt, xap=xap, xres=xres):
                    op("vector", lambda e, p=bt, xa=xap: e.tensor_tensor(
                        xa, xa, p.ap(), OP.add),
                       rd_res=(rbt[0], rbt[1]), wr_res=(xres,), mark=True)
                if wpend is not None:
                    wpend()
                wpend = wepi
            wpend()
            wt = [ic * 4 + tl for tl in range(4)]
            ln_stats_batch(wt)
            ffn_run(s, wt)
            for tau in wt:
                dma(y_hbm[s * N + tau * 128:s * N + (tau + 1) * 128, :],
                    x_tok[:, tau * D:(tau + 1) * D],
                    rd_res=(r_x[tau],), eng="gpsimd")

        for s in range(SEQ):
            if s > 0:
                for tau in range(NT):
                    dma(x_tok[:, tau * D:(tau + 1) * D],
                        x_hbm[s * N + tau * 128:s * N + (tau + 1) * 128, :],
                        wr_res=(r_x[tau],))
            ln_stats_batch(list(range(NT)))
            ffn_run(s, list(range(NT)), hook=setup_hook if s == 0 else None)
            ln_stats_batch(list(range(NT)), need_nmr=True)
            for ic in range(NWIN):
                for tau in range(ic * 4, ic * 4 + 4):
                    ln2_tile(s, tau)
                kv_chunk(s, ic)
                attn_window(s, ic)

        total_st = b.cnt["st"]
        b.emit("gpsimd", lambda e: e.wait_ge(sems["st"], total_st), ())

        def replay(name):
            def f(eng):
                for waits, bld, inc in b.ops[name]:
                    for (sname, v) in waits:
                        eng.wait_ge(sems[sname], v)
                    inst = bld(eng)
                    if inc is not None:
                        inst.then_inc(sems[inc], 16 if inc in DMA_SEMS else 1)
            return f

        block.sync(replay("sync"))
        block.tensor(replay("tensor"))
        block.vector(replay("vector"))
        block.scalar(replay("scalar"))
        block.gpsimd(replay("gpsimd"))

    return nc


_NC_CACHE = None


def kernel(L0_attn_x, L0_attn_Wq, L0_attn_Wkv, L0_attn_Wout, L0_attn_bout,
           L0_attn_rel_emb, L0_ffn_weight, L0_ffn_bias):
    global _NC_CACHE
    ohd, ident32, identbf = _host_consts()
    if _NC_CACHE is None:
        _NC_CACHE = build_nc()
    nc = _NC_CACHE
    x = np.ascontiguousarray(np.asarray(L0_attn_x, dtype=np.float32))
    base = dict(
        wq=np.ascontiguousarray(np.asarray(L0_attn_Wq, np.float32)),
        wkv=np.ascontiguousarray(np.asarray(L0_attn_Wkv, np.float32)),
        wout=np.ascontiguousarray(np.asarray(L0_attn_Wout, np.float32)),
        bout=np.asarray(L0_attn_bout, np.float32).reshape(1, D),
        remb=np.ascontiguousarray(np.asarray(L0_attn_rel_emb, np.float32)),
        fw=np.ascontiguousarray(np.asarray(L0_ffn_weight, np.float32)),
        fb=np.asarray(L0_ffn_bias, np.float32).reshape(1, D),
        ohd=ohd, id32=ident32, idbf=identbf,
    )
    in_maps = []
    for c in range(N_CORE):
        m = dict(base)
        m["x"] = np.ascontiguousarray(
            x[c * SEQ:(c + 1) * SEQ].reshape(SEQ * N, D))
        in_maps.append(m)
    res = run_bass_kernel_spmd(nc, in_maps, core_ids=list(range(N_CORE)))
    outs = [np.asarray(r["y"], dtype=np.float32).reshape(SEQ, N, D)
            for r in res.results]
    return np.concatenate(outs, axis=0)


# version 32 (final)
# speedup vs baseline: 1.2079x; 1.2079x over previous
import numpy as np
import ml_dtypes
from contextlib import ExitStack

import concourse.bass as bass
import concourse.mybir as mybir
from concourse.bass_utils import run_bass_kernel_spmd

F32 = mybir.dt.float32
BF16 = mybir.dt.bfloat16
AF = mybir.ActivationFunctionType
OP = mybir.AluOpType
AX = mybir.AxisListType

N_CORE = 8
SEQ = 2
N = 2048
D = 1024
NT = N // 128
NC = D // 128
HEADS = 16
DH = 64
WIN = 512
NWIN = N // WIN

ENGINES = ("sync", "tensor", "vector", "scalar", "gpsimd")
DMA_SEMS = {"ld", "st"}
SEMS = ("ld", "st", "pe", "dve", "act")
ENG_SEM = {"sync": "ld", "tensor": "pe", "vector": "dve", "scalar": "act", "gpsimd": "st"}


def _bucket_of(n):
    if n < 16:
        return n
    nf = np.float32(max(n, 1))
    v = np.float32(np.log(np.float32(nf / np.float32(16.0))))
    v = np.float32(v / np.float32(np.log(8.0)))
    v = np.float32(v * np.float32(16.0))
    val = 16 + int(v)
    return min(val, 31)


def _host_consts():
    ohd = np.zeros((32, 384), dtype=np.float32)
    for x in range(384):
        d = x - 127
        b = _bucket_of(d) if d >= 0 else 0
        ohd[b, x] = 1.0
    ident32 = np.eye(128, dtype=np.float32)
    identbf = np.eye(128, dtype=np.float32).astype(ml_dtypes.bfloat16)
    return ohd, ident32, identbf


class Ev:
    __slots__ = ("sem", "val", "force")

    def __init__(self, sem, val, force=False):
        self.sem = sem
        self.val = val
        self.force = force


class Res:

    __slots__ = ("w", "weng", "rs", "tiny")

    def __init__(self):
        self.w = None
        self.weng = None
        self.rs = {}
        self.tiny = False


class B:
    def __init__(self):
        self.ops = {e: [] for e in ENGINES}
        self.cnt = {s: 0 for s in SEMS}
        self.lastw = {e: {s: 0 for s in SEMS} for e in ENGINES}

    def emit(self, eng, builder, waits=(), mark=False):
        sem = ENG_SEM[eng]
        real = []
        for ev in waits:
            if ev is None:
                continue
            if ev.sem == sem and not ev.force:
                continue
            if ev.val > self.lastw[eng][ev.sem]:
                real.append((ev.sem, ev.val))
                self.lastw[eng][ev.sem] = ev.val
        ev = None
        if mark:
            amt = 16 if sem in DMA_SEMS else 1
            self.cnt[sem] += amt
            ev = Ev(sem, self.cnt[sem])
        self.ops[eng].append((real, builder, sem if mark else None))
        return ev

    def rd(self, eng, *ress):
        waits = []
        for r in ress:
            if r.w is not None and r.weng != eng:
                waits.append(r.w)
            elif r.w is None and r.weng is not None and r.weng != eng:
                raise RuntimeError("cross-engine read of unmarked write")
            elif r.weng == eng and r.tiny:
                if r.w is None:
                    raise RuntimeError("tiny write must be marked")
                waits.append(Ev(r.w.sem, r.w.val, force=True))
        return waits

    def wr(self, eng, *ress):
        waits = []
        for r in ress:
            if r.w is not None and r.weng != eng:
                waits.append(r.w)
            for reng, (rev, dangling) in r.rs.items():
                if reng != eng:
                    if dangling or rev is None:
                        raise RuntimeError("WAR on unmarked read")
                    waits.append(rev)
        return waits

    def did_rd(self, eng, ev, *ress):
        for r in ress:
            cur = r.rs.get(eng)
            if ev is None:
                if cur is None:
                    r.rs[eng] = [None, True]
                else:
                    cur[1] = True
            else:
                r.rs[eng] = [ev, False]

    def did_wr(self, eng, ev, *ress, tiny=False):
        for r in ress:
            r.w = ev
            r.weng = eng
            r.rs = {}
            r.tiny = tiny


def build_nc():
    nc = bass.Bass("TRN2", target_bir_lowering=False)
    b = B()

    x_hbm = nc.dram_tensor("x", [SEQ * N, D], F32, kind="ExternalInput").ap()
    wq_hbm = nc.dram_tensor("wq", [D, D], F32, kind="ExternalInput").ap()
    wkv_hbm = nc.dram_tensor("wkv", [D, 2 * DH], F32, kind="ExternalInput").ap()
    wout_hbm = nc.dram_tensor("wout", [D, D], F32, kind="ExternalInput").ap()
    bout_hbm = nc.dram_tensor("bout", [1, D], F32, kind="ExternalInput").ap()
    remb_hbm = nc.dram_tensor("remb", [32, HEADS], F32, kind="ExternalInput").ap()
    fw_hbm = nc.dram_tensor("fw", [D, D], F32, kind="ExternalInput").ap()
    fb_hbm = nc.dram_tensor("fb", [1, D], F32, kind="ExternalInput").ap()
    ohd_hbm = nc.dram_tensor("ohd", [32, 384], F32, kind="ExternalInput").ap()
    id32_hbm = nc.dram_tensor("id32", [128, 128], F32, kind="ExternalInput").ap()
    idbf_hbm = nc.dram_tensor("idbf", [128, 128], BF16, kind="ExternalInput").ap()
    y_hbm = nc.dram_tensor("y", [SEQ * N, D], F32, kind="ExternalOutput").ap()

    with ExitStack() as st_:
        E = st_.enter_context
        x_tok = E(nc.sbuf_tensor("x_tok", [128, NT * D], F32))
        xln2T = E(nc.sbuf_tensor("xln2T", [128, NC * N], BF16))
        wq_sb = E(nc.sbuf_tensor("wq_sb", [128, NC * D], BF16))
        wout_sb = E(nc.sbuf_tensor("wout_sb", [128, NC * D], BF16))
        wdT = E(nc.sbuf_tensor("wdT", [128, NC * D], BF16))
        qf = E(nc.sbuf_tensor("qf", [128, 8 * WIN], BF16))
        afeat = E(nc.sbuf_tensor("afeat", [128, 8 * WIN], BF16))
        eb = E(nc.sbuf_tensor("eb", [128, HEADS * 256], BF16))
        attn_t = E(nc.sbuf_tensor("attn_t", [128, 4 * WIN], BF16))
        kdup = E(nc.sbuf_tensor("kdup", [128, N], BF16))
        vaug = E(nc.sbuf_tensor("vaug", [128, NT * 128], BF16))
        wkv_sb = E(nc.sbuf_tensor("wkv_sb", [128, NC * 128], BF16))
        xmz = E(nc.sbuf_tensor("xmz", [128, D], F32))
        azdb = E(nc.sbuf_tensor("azdb", [128, D], BF16))
        dadd = E(nc.sbuf_tensor("dadd", [128, D], BF16))
        xdp = [E(nc.sbuf_tensor(f"xd{i}", [128, D], BF16)) for i in range(2)]
        _xdt0 = E(nc.sbuf_tensor("xdt0", [128, D], BF16))
        xdt = [_xdt0, _xdt0]
        rsb = E(nc.sbuf_tensor("rsb", [128, WIN], F32))
        vstg = rsb.ap().bitcast(BF16)
        cols = E(nc.sbuf_tensor("cols", [128, 400], F32))
        _xl_f32 = None
        remb_sb = E(nc.sbuf_tensor("remb_sb", [32, 16], F32))
        remb8 = E(nc.sbuf_tensor("remb8", [32, 16], F32))
        id32 = E(nc.sbuf_tensor("id32_sb", [128, 128], F32))
        idbf = E(nc.sbuf_tensor("idbf_sb", [128, 128], BF16))
        onesb = E(nc.sbuf_tensor("onesb", [1, 128], BF16))
        onesf = E(nc.sbuf_tensor("onesf", [1, 128], F32))
        bout_sb = E(nc.sbuf_tensor("bout_sb", [1, D], BF16))
        hbias_sb = E(nc.sbuf_tensor("hbias_sb", [1, D], BF16))
        negc_col = E(nc.sbuf_tensor("negc_col", [128, 1], F32))
        ccol = E(nc.sbuf_tensor("ccol", [128, 3], F32))
        scal = E(nc.sbuf_tensor("scal", [1, 16], F32))
        bvstage = E(nc.sbuf_tensor("bvstage", [128, 16], F32))
        pS4 = E(nc.psum_tensor("pS4", [128, 2048], F32))
        pS = [pS4.ap()[:, i * 512:(i + 1) * 512] for i in range(4)]
        pX = [pS[2], pS[3]]
        _pS4bf = pS4.ap().bitcast(BF16)
        pT = [_pS4bf[:, 0:1024], _pS4bf[:, 1024:2048]]
        pava = E(nc.psum_tensor("pava", [128, 1024], F32))
        pavb = E(nc.psum_tensor("pavb", [128, 1024], F32))
        pav = [pava.ap()[:, 0:512], pava.ap()[:, 512:1024],
               pavb.ap()[:, 0:512], pavb.ap()[:, 512:1024]]
        sems = {s: E(nc.semaphore(s)) for s in SEMS}
        block = E(nc.Block())

        qf_f32 = qf.ap().bitcast(F32)
        _xl32 = xln2T.ap().bitcast(F32)
        ohd_sb = _xl32[0:32, 0:384]
        bvT = _xl32[0:16, 384:768]
        expbv = xln2T.ap()[0:16, 1536:1920]

        R = lambda: Res()
        r_x = [R() for _ in range(NT)]
        r_xln2T = R()
        r_wq, r_wout, r_wdT = R(), R(), R()
        r_qf, r_afeat, r_eb = R(), R(), R()
        r_attn = [R() for _ in range(4)]
        r_kdup = [R() for _ in range(4)]
        r_vaug = [R() for _ in range(4)]
        r_wkv, r_vstg = R(), R()
        r_xmz, r_azdb, r_dadd = R(), R(), R()
        r_xdp = [R(), R()]
        _rxdt = R()
        r_xdt = [_rxdt, _rxdt]
        r_rsb = R()
        r_cols = [R(), R()]
        r_negc = R()
        r_scal = R()
        r_pS = [R(), R(), R(), R()]
        r_pav = [R() for _ in range(4)]
        r_pT = [r_pS[0], r_pS[1]]
        r_pX = [r_pS[2], r_pS[3]]
        r_stage = [R(), R()]
        r_bv = R()
        r_expbv = R()
        r_id32, r_idbf, r_ohd, r_remb, r_remb8 = R(), R(), R(), R(), R()
        r_onesb, r_onesf, r_bout, r_hbias, r_nc31 = R(), R(), R(), R(), R()
        r_ccol = R()

        def dma(dst, src, waits=(), rd_res=(), wr_res=(), eng="sync"):
            waits = list(waits) + b.rd(eng, *rd_res) + b.wr(eng, *wr_res)
            ev = b.emit(eng, lambda e, d=dst, s=src: e.dma_start(out=d, in_=s),
                        waits, mark=True)
            b.did_rd(eng, ev, *rd_res)
            b.did_wr(eng, ev, *wr_res)
            return ev

        def op(eng, builder, rd_res=(), wr_res=(), waits=(), mark=False, tiny=False):
            if tiny:
                mark = True
            waits = list(waits) + b.rd(eng, *rd_res) + b.wr(eng, *wr_res)
            ev = b.emit(eng, builder, waits, mark=mark)
            b.did_rd(eng, ev, *rd_res)
            b.did_wr(eng, ev, *wr_res, tiny=tiny)
            return ev

        dma(id32[:], id32_hbm, wr_res=(r_id32,))
        dma(idbf[:], idbf_hbm, wr_res=(r_idbf,))
        dma(ohd_sb[:], ohd_hbm, wr_res=(r_ohd,))
        dma(remb_sb[:], remb_hbm, wr_res=(r_remb,))

        op("vector", lambda e: e.memset(onesb[:], 1.0), wr_res=(r_onesb,), mark=True)
        op("vector", lambda e: e.memset(onesf[:], 1.0), wr_res=(r_onesf,), mark=True)
        op("vector", lambda e: e.memset(ccol[:, 0:1], 1e-5), wr_res=(r_ccol,), mark=True)
        op("vector", lambda e: e.memset(ccol[:, 1:2], -0.5), wr_res=(r_ccol,), mark=True)
        op("vector", lambda e: e.memset(ccol[:, 2:3], -1.0), wr_res=(r_ccol,), mark=True)
        va_ones = vaug[:].rearrange("p (t c) -> p t c", c=128)[:, :, 64:128]
        ev_vaones = op("vector", lambda e: e.memset(va_ones, 281474976710656.0),
                       wr_res=tuple(r_vaug), mark=True)

        def load_cast(dst_sb, dst_res, hbm, ncols, scale, out_bf=True, quant=False):
            for kc in range(NC):
                stg = qf_f32[:, (kc % 2) * 1024:(kc % 2) * 1024 + ncols]
                rstg = r_stage[kc % 2]
                dma(stg, hbm[kc * 128:(kc + 1) * 128, :], wr_res=(rstg,))
                dst = dst_sb[:, kc * ncols:(kc + 1) * ncols]
                if scale == 1.0:
                    ev = op("vector", lambda e, d=dst, s=stg: e.tensor_copy(d, s),
                            rd_res=(rstg,), wr_res=(dst_res,), mark=True)
                else:
                    ev = op("vector",
                            lambda e, d=dst, s=stg, sc=scale: e.tensor_scalar(
                                d, s, sc, None, OP.mult),
                            rd_res=(rstg,), wr_res=(dst_res,), mark=True)
            return ev

        stgrow = qf_f32[0:1, 0:D]
        dma(stgrow, bout_hbm, wr_res=(r_stage[0],))
        op("vector", lambda e: e.tensor_copy(bout_sb[:], stgrow),
           rd_res=(r_stage[0],), wr_res=(r_bout,), mark=True)
        dma(stgrow, fb_hbm, wr_res=(r_stage[0],))
        op("vector", lambda e: e.tensor_scalar(hbias_sb[:], stgrow, 0.5, None, OP.mult),
           rd_res=(r_stage[0],), wr_res=(r_hbias,), mark=True)

        def quant_tile(src_f32_ap, src_res, nmean_col, rstd75_col, rstd8_col,
                       out_bf_ap, out_res, colpar):
            rc = r_cols[colpar]
            c_bmax = cols[:, colpar * 128 + 0:colpar * 128 + 32]
            c_xs = cols[:, colpar * 128 + 32:colpar * 128 + 64]
            c_rxs = cols[:, colpar * 128 + 64:colpar * 128 + 96]
            c_rr8 = cols[:, colpar * 128 + 96:colpar * 128 + 128]
            if nmean_col is not None:
                op("scalar", lambda e: e.activation(
                    xmz[:], src_f32_ap, AF.Identity, bias=nmean_col, scale=1.0),
                   rd_res=(src_res, r_ln), wr_res=(r_xmz,), mark=True)
                zsrc, zres = xmz[:], r_xmz
            else:
                zsrc, zres = src_f32_ap, src_res
            op("vector", lambda e: e.tensor_reduce(
                c_bmax, zsrc.rearrange("p (b k) -> p b k", k=32), AX.X, OP.max,
                apply_absolute_value=True),
               rd_res=(zres,), wr_res=(rc,), tiny=True)
            if nmean_col is not None:
                op("vector", lambda e: e.tensor_scalar(
                    c_xs, c_bmax, rstd75_col, 1e-30, OP.mult, OP.max),
                   rd_res=(rc, r_ln), wr_res=(rc,), tiny=True)
                op("vector", lambda e: e.reciprocal(c_rxs, c_xs),
                   rd_res=(rc,), wr_res=(rc,), tiny=True)
                op("vector", lambda e: e.tensor_scalar(
                    c_rr8, c_rxs, rstd8_col, None, OP.mult),
                   rd_res=(rc, r_ln), wr_res=(rc,), tiny=True)
                final_scale = 1.0 / 16.0
            else:
                op("vector", lambda e: e.tensor_scalar(
                    c_xs, c_bmax, 1.0 / 7.5, 1e-30, OP.mult, OP.max),
                   rd_res=(rc,), wr_res=(rc,), tiny=True)
                op("vector", lambda e: e.reciprocal(c_rxs, c_xs),
                   rd_res=(rc,), wr_res=(rc,), tiny=True)
                op("vector", lambda e: e.tensor_scalar(
                    c_rr8, c_rxs, 8.0, None, OP.mult),
                   rd_res=(rc,), wr_res=(rc,), tiny=True)
                final_scale = 1.0 / 8.0
            c_xsf = c_rxs
            op("vector", lambda e: e.tensor_scalar(
                c_xsf, c_xs, final_scale, None, OP.mult),
               rd_res=(rc,), wr_res=(rc,), tiny=True)
            bc = lambda colap: colap.unsqueeze(2).broadcast_to([128, 32, 1]).broadcast_to([128, 32, 32])
            z3 = zsrc.rearrange("p (b k) -> p b k", k=32)
            ev_z = op("vector", lambda e: e.tensor_tensor(z3, z3, bc(c_rr8), OP.mult),
                      rd_res=(rc,), wr_res=(zres,), mark=True)
            ev_az = op("vector", lambda e: e.scalar_tensor_tensor(
                azdb[:], zsrc, -1.0, zsrc, OP.mult, OP.max),
                       rd_res=(zres,), wr_res=(r_azdb,), mark=True)
            op("vector", lambda e: e.tensor_scalar(
                dadd[:], azdb[:], 16.0, 192.0, OP.is_ge, OP.mult),
               rd_res=(r_azdb,), wr_res=(r_dadd,), mark=True)
            op("vector", lambda e: e.tensor_scalar(
                azdb[:], azdb[:], 32.0, 384.0, OP.is_ge, OP.mult),
               rd_res=(r_azdb,), wr_res=(r_azdb,), mark=True)
            op("vector", lambda e: e.tensor_tensor(dadd[:], dadd[:], azdb[:], OP.add),
               rd_res=(r_azdb,), wr_res=(r_dadd,), mark=True)
            op("vector", lambda e, u=out_bf_ap: e.scalar_tensor_tensor(
                u, zsrc, 192.0, dadd[:], OP.add, OP.add),
               rd_res=(zres, r_dadd), wr_res=(out_res,), mark=True)
            op("vector", lambda e, u=out_bf_ap: e.scalar_tensor_tensor(
                azdb[:], u, -192.0, dadd[:], OP.add, OP.subtract),
               rd_res=(out_res, r_dadd), wr_res=(r_azdb,), mark=True)
            ev = op("vector", lambda e: e.tensor_tensor(
                out_bf_ap.rearrange("p (b k) -> p b k", k=32),
                azdb[:].rearrange("p (b k) -> p b k", k=32), bc(c_xsf), OP.mult),
                rd_res=(r_azdb, rc), wr_res=(out_res,), mark=True)
            return ev

        for jc in range(NC):
            stg = qf_f32[:, (jc % 2) * 1024:(jc % 2) * 1024 + D]
            rstg = r_stage[jc % 2]
            dma(stg, fw_hbm[jc * 128:(jc + 1) * 128, :], wr_res=(rstg,))
            quant_tile(stg, rstg, None, None, None, xdp[jc % 2][:],
                       r_xdp[jc % 2], jc % 2)
            pt, rpt = pT[jc % 2], r_pT[jc % 2]
            xdl, rxdl = xdp[jc % 2], r_xdp[jc % 2]
            for kc in range(NC):
                op("tensor", lambda e, k=kc, p=pt, xx=xdl: e.transpose(
                    p[:, k * 128:(k + 1) * 128],
                    xx[:, k * 128:(k + 1) * 128], idbf[:]),
                   rd_res=(rxdl, r_idbf), wr_res=(rpt,),
                   mark=(kc == NC - 1))
            dst = wdT[:].rearrange("p (k j) -> p k j", j=D)[:, :, jc * 128:(jc + 1) * 128]
            src = pt.rearrange("p (k t) -> p k t", t=128)
            op("vector", lambda e, d=dst, s=src: e.tensor_copy(d, s),
               rd_res=(rpt,), wr_res=(r_wdT,), mark=True)

        for tau in range(NT):
            dma(x_tok[:, tau * D:(tau + 1) * D],
                x_hbm[tau * 128:(tau + 1) * 128, :], wr_res=(r_x[tau],))

        r_ebpre = [R(), R()]

        def load_cast_chunk(dst_sb, dst_res, hbm, kc, scale):
            stg = qf_f32[:, (kc % 2) * 1024:(kc % 2) * 1024 + D]
            rstg = r_stage[kc % 2]
            dma(stg, hbm[kc * 128:(kc + 1) * 128, :], wr_res=(rstg,))
            dst = dst_sb[:, kc * D:(kc + 1) * D]
            if scale == 1.0:
                op("vector", lambda e, d=dst, s2=stg: e.tensor_copy(d, s2),
                   rd_res=(rstg,), wr_res=(dst_res,), mark=True)
            else:
                op("vector", lambda e, d=dst, s2=stg, sc=scale: e.tensor_scalar(
                    d, s2, sc, None, OP.mult),
                   rd_res=(rstg,), wr_res=(dst_res,), mark=True)

        def bias_round(ch):
            op("tensor", lambda e, c=ch: e.matmul(
                pS[2][:, 0:16], ohd_sb[:, c * 128:(c + 1) * 128], remb8[:],
                start=True, stop=True),
               rd_res=(r_ohd, r_remb8), wr_res=(r_pS[2],), mark=True)
            op("vector", lambda e: e.tensor_copy(bvstage[:], pS[2][:, 0:16]),
               rd_res=(r_pS[2],), wr_res=(r_bv,), mark=True)
            op("tensor", lambda e: e.transpose(
                pS[3][0:16, 0:128], bvstage[:], id32[:]),
               rd_res=(r_bv, r_id32), wr_res=(r_pS[3],), mark=True)
            op("vector", lambda e, c=ch: e.tensor_copy(
                bvT[:, c * 128:(c + 1) * 128], pS[3][0:16, 0:128]),
               rd_res=(r_pS[3],), wr_res=(r_bv,), mark=True)

        def emit_eb_head(h):
            par = h % 2
            ebt = xln2T.ap().tensor
            eboff = 1536
            ebpre = afeat[:, par * 256:par * 256 + 256]
            for g in range(4):
                rowlen = xln2T.ap().shape[1]
                src = bass.AP(ebt, h * rowlen + eboff + 32 * g,
                              [[rowlen, 1], [1, 32], [1, 256]])
                dma(ebpre[32 * (3 - g):32 * (4 - g), :], src,
                    rd_res=(r_expbv,), wr_res=(r_ebpre[par],))
            op("vector", lambda e, hh=h, ep=ebpre: e.stream_shuffle(
                eb[:, hh * 256:(hh + 1) * 256], ep, list(range(31, -1, -1))),
               rd_res=(r_ebpre[par], r_afeat), wr_res=(r_eb,), mark=True)

        def setup_hook(i):
            if i == 0:
                op("vector", lambda e: e.tensor_scalar(
                    remb8[:], remb_sb[:], 8.0, None, OP.mult),
                   rd_res=(r_remb,), wr_res=(r_remb8,), mark=True)
                bias_round(0)
            elif i in (1, 2):
                bias_round(i)
            elif i == 3:
                op("vector", lambda e: e.tensor_scalar(
                    cols[0:16, 386:387], bvT[:, 327:328], -1.0, None, OP.mult),
                   rd_res=(r_bv,), wr_res=(r_nc31,), mark=True)
                op("scalar", lambda e: e.activation(
                    expbv[:], bvT[:], AF.Exp, bias=cols[0:16, 386:387], scale=1.0),
                   rd_res=(r_bv, r_nc31), wr_res=(r_expbv,), mark=True)
                op("vector", lambda e: e.memset(expbv[:, 0:127], 0.0),
                   wr_res=(r_expbv,), mark=True)
            elif i <= 11:
                emit_eb_head(2 * (i - 4))
                emit_eb_head(2 * (i - 4) + 1)
            else:
                for k in range(4):
                    g = 4 * (i - 12) + k
                    if g < 8:
                        load_cast_chunk(wq_sb, r_wq, wq_hbm, g, 0.125)
                    else:
                        load_cast_chunk(wout_sb, r_wout, wout_hbm, g - 8, 1.0)
                for k2 in range(2):
                    kc = 2 * (i - 12) + k2
                    stg = rsb[:, (kc % 2) * 128:(kc % 2) * 128 + 128]
                    dma(stg, wkv_hbm[kc * 128:(kc + 1) * 128, :],
                        wr_res=(r_vstg, r_rsb))
                    op("vector", lambda e, d=wkv_sb[:, kc * 128:(kc + 1) * 128],
                       s2=stg: e.tensor_copy(d, s2),
                       rd_res=(r_vstg,), wr_res=(r_wkv,), mark=True)

        CB = 256
        A_SUM, A_SQ, A_MEAN, A_VAR, A_RSTD, A_R75, A_R8, A_NMR = range(8)
        r_ln = R()

        def c_a(area, t0, n=1):
            return cols[:, CB + area * 16 + t0:CB + area * 16 + t0 + n]

        def ln_stats_batch(taus, need_nmr=False):
            t0, n = taus[0], len(taus)
            for tau in taus:
                xap = x_tok[:, tau * D:(tau + 1) * D]
                xres = r_x[tau]
                op("scalar", lambda e, xa=xap, t=tau: e.activation(
                    xmz[:], xa, AF.Identity, accum_out=c_a(A_SUM, t)),
                   rd_res=(xres,), wr_res=(r_xmz, r_ln), mark=True)
                op("scalar", lambda e, xa=xap, t=tau: e.activation(
                    xmz[:], xa, AF.Square, accum_out=c_a(A_SQ, t)),
                   rd_res=(xres,), wr_res=(r_xmz, r_ln), mark=True)
            op("vector", lambda e, a=t0, k=n: e.tensor_scalar(
                c_a(A_MEAN, a, k), c_a(A_SUM, a, k), 1.0 / D, None, OP.mult),
               rd_res=(r_ln,), wr_res=(r_ln,), tiny=True)
            op("vector", lambda e, a=t0, k=n: e.tensor_scalar(
                c_a(A_SUM, a, k), c_a(A_MEAN, a, k), -1.0, None, OP.mult),
               rd_res=(r_ln,), wr_res=(r_ln,), tiny=True)
            op("vector", lambda e, a=t0, k=n: e.tensor_scalar(
                c_a(A_VAR, a, k), c_a(A_SQ, a, k), 1.0 / D, None, OP.mult),
               rd_res=(r_ln,), wr_res=(r_ln,), tiny=True)
            op("vector", lambda e, a=t0, k=n: e.scalar_tensor_tensor(
                c_a(A_SQ, a, k), c_a(A_MEAN, a, k), 1.0, c_a(A_MEAN, a, k),
                OP.mult, OP.mult),
               rd_res=(r_ln,), wr_res=(r_ln,), tiny=True)
            ev = op("vector", lambda e, a=t0, k=n: e.tensor_tensor(
                c_a(A_VAR, a, k), c_a(A_VAR, a, k), c_a(A_SQ, a, k), OP.subtract),
                rd_res=(r_ln,), wr_res=(r_ln,), tiny=True)
            op("scalar", lambda e, a=t0, k=n: e.activation(
                c_a(A_RSTD, a, k), c_a(A_VAR, a, k), AF.Ln, bias=ccol[:, 0:1],
                scale=1.0),
               rd_res=(r_ln, r_ccol), wr_res=(r_ln,), waits=[ev], mark=True)
            ev2 = op("scalar", lambda e, a=t0, k=n: e.activation(
                c_a(A_RSTD, a, k), c_a(A_RSTD, a, k), AF.Exp, bias=0.0,
                scale=ccol[:, 1:2]),
                rd_res=(r_ln, r_ccol), wr_res=(r_ln,), mark=True)
            op("vector", lambda e, a=t0, k=n: e.tensor_scalar(
                c_a(A_R75, a, k), c_a(A_RSTD, a, k), 1.0 / 7.5, None, OP.mult),
               rd_res=(r_ln,), wr_res=(r_ln,), waits=[ev2], tiny=True)
            op("vector", lambda e, a=t0, k=n: e.tensor_scalar(
                c_a(A_R8, a, k), c_a(A_RSTD, a, k), 8.0, None, OP.mult),
               rd_res=(r_ln,), wr_res=(r_ln,), tiny=True)
            if need_nmr:
                op("vector", lambda e, a=t0, k=n: e.scalar_tensor_tensor(
                    c_a(A_NMR, a, k), c_a(A_MEAN, a, k), -1.0, c_a(A_RSTD, a, k),
                    OP.mult, OP.mult),
                   rd_res=(r_ln,), wr_res=(r_ln,), tiny=True)

        def ffn_tile_mm(s, tau, xt_vec=False):
            par = tau % 2
            xap = x_tok[:, tau * D:(tau + 1) * D]
            xres = r_x[tau]
            quant_tile(xap, xres, c_a(A_SUM, tau), c_a(A_R75, tau),
                       c_a(A_R8, tau), xdp[par][:], r_xdp[par], par)
            pt, rpt = pT[par], r_pT[par]
            for kc in range(NC):
                op("tensor", lambda e, k=kc, p=pt, xx=xdp[par]: e.transpose(
                    p[:, k * 128:(k + 1) * 128], xx[:, k * 128:(k + 1) * 128],
                    idbf[:]),
                   rd_res=(r_xdp[par], r_idbf), wr_res=(rpt,), mark=(kc == NC - 1))
            xt, rxt = xdt[par], r_xdt[par]
            if xt_vec:
                op("vector", lambda e, p=pt, d=xt: e.tensor_copy(d[:], p),
                   rd_res=(rpt,), wr_res=(rxt,), mark=True)
            else:
                op("scalar", lambda e, p=pt, d=xt: e.copy(d[:], p),
                   rd_res=(rpt,), wr_res=(rxt,), mark=True)
            bt = pava if par == 0 else pavb
            rbt = [r_pav[0], r_pav[1]] if par == 0 else [r_pav[2], r_pav[3]]
            banks = [(bt.ap()[:, 0:512], rbt[0]), (bt.ap()[:, 512:1024], rbt[1])]
            for kc in range(NC):
                for half in range(2):
                    ps, rps = banks[half]
                    op("tensor", lambda e, k=kc, h=half, p=ps, xx=xt: e.matmul(
                        p, xx[:, k * 128:(k + 1) * 128],
                        wdT[:, k * D + h * 512:k * D + (h + 1) * 512],
                        start=(k == 0), stop=False),
                       rd_res=(rxt, r_wdT), wr_res=(rps,),
                       mark=(kc == NC - 1))
            for half in range(2):
                ps, rps = banks[half]
                op("tensor", lambda e, h=half, p=ps: e.matmul(
                    p, onesb[0:1, :], hbias_sb[0:1, h * 512:(h + 1) * 512],
                    start=False, stop=True),
                   rd_res=(r_onesb, r_hbias), wr_res=(rps,), mark=True)

            def epilogue():
                op("vector", lambda e, p=bt, xa=xap: e.tensor_tensor(
                    xa, xa, p.ap(), OP.add),
                   rd_res=(rbt[0], rbt[1]), wr_res=(xres,), mark=True)
            return epilogue

        def ffn_run(s, taus, hook=None, xt_vec=False):
            pend = None
            for i, tau in enumerate(taus):
                nxt = ffn_tile_mm(s, tau, xt_vec=xt_vec)
                if pend is not None:
                    pend()
                pend = nxt
                if hook is not None:
                    hook(i)
            pend()

        def ln2_tile(s, tau):
            par = tau % 2
            xap = x_tok[:, tau * D:(tau + 1) * D]
            xres = r_x[tau]
            xl, rxl = xdp[par], r_xdp[par]
            op("scalar", lambda e, t=tau, d=xl: e.activation(
                d[:], xap, AF.Identity, bias=c_a(A_NMR, t), scale=c_a(A_RSTD, t)),
               rd_res=(xres, r_ln), wr_res=(rxl,), mark=True)
            pt, rpt = pT[par], r_pT[par]
            for kc in range(NC):
                op("tensor", lambda e, k=kc, p=pt, xx=xl: e.transpose(
                    p[:, k * 128:(k + 1) * 128], xx[:, k * 128:(k + 1) * 128], idbf[:]),
                   rd_res=(rxl, r_idbf), wr_res=(rpt,), mark=(kc == NC - 1))
            dst = xln2T[:].rearrange("p (k t) -> p k t", t=N)[:, :, tau * 128:(tau + 1) * 128]
            src = pt.rearrange("p (k t) -> p k t", t=128)
            op("vector", lambda e, d=dst, sr=src: e.tensor_copy(d, sr),
               rd_res=(rpt,), wr_res=(r_xln2T, r_ohd, r_bv, r_expbv), mark=True)

        def kv_chunk(s, tc):
            ps, rps = pav[tc % 2], r_pav[tc % 2]
            for kc in range(NC):
                op("tensor", lambda e, k=kc, t=tc, p=ps: e.matmul(
                    p, wkv_sb[:, k * 128:(k + 1) * 128],
                    xln2T[:, k * N + t * 512:k * N + (t + 1) * 512],
                    start=(k == 0), stop=(k == NC - 1)),
                   rd_res=(r_wkv, r_xln2T), wr_res=(rps,),
                   mark=(kc == NC - 1))
            op("scalar", lambda e, t=tc, p=ps: e.copy(
                kdup[0:64, t * 512:(t + 1) * 512], p[0:64, :]),
               rd_res=(rps,), wr_res=(r_kdup[tc],), mark=True)
            op("scalar", lambda e, t=tc, p=ps: e.copy(
                kdup[64:128, t * 512:(t + 1) * 512], p[0:64, :]),
               rd_res=(rps,), wr_res=(r_kdup[tc],), mark=True)
            op("scalar", lambda e, p=ps: e.copy(
                vstg[0:64, 0:512], p[64:128, :]),
               rd_res=(rps,), wr_res=(r_vstg,), mark=True)
            for jl in range(4):
                op("tensor", lambda e, j=jl: e.transpose(
                    pT[0][:, j * 64:(j + 1) * 64], vstg[0:64, j * 128:(j + 1) * 128],
                    idbf[0:64, 0:64]),
                   rd_res=(r_vstg, r_idbf), wr_res=(r_pT[0],), mark=(jl == 3))
            dst = vaug[:].rearrange("p (t c) -> p t c", c=128)[:, tc * 4:tc * 4 + 4, 0:64]
            src = pT[0][:, 0:256].rearrange("p (t c) -> p t c", c=64)
            op("vector", lambda e, d=dst, sr=src: e.tensor_copy(d, sr),
               rd_res=(r_pT[0],), wr_res=(r_vaug[tc],), mark=True)
            op("vector", lambda e, t=tc: e.tensor_reduce(
                cols[0:64, 384:385], kdup[0:64, t * 512:(t + 1) * 512], AX.X,
                OP.max, apply_absolute_value=True),
               rd_res=(r_kdup[tc],), wr_res=(r_scal,), tiny=True)
            op("tensor", lambda e: e.transpose(
                pS[0][0:1, 0:64], cols[0:64, 384:385], id32[0:64, 0:64]),
               rd_res=(r_scal, r_id32), wr_res=(r_pS[0],), mark=True)
            if tc == 0:
                op("vector", lambda e: e.tensor_reduce(
                    scal[0:1, 0:1], pS[0][0:1, 0:64], AX.X, OP.max),
                   rd_res=(r_pS[0],), wr_res=(r_scal,), tiny=True)
            else:
                op("vector", lambda e: e.tensor_reduce(
                    scal[0:1, 3:4], pS[0][0:1, 0:64], AX.X, OP.max),
                   rd_res=(r_pS[0],), wr_res=(r_scal,), tiny=True)
                op("vector", lambda e: e.tensor_tensor(
                    scal[0:1, 0:1], scal[0:1, 0:1], scal[0:1, 3:4], OP.max),
                   rd_res=(r_scal,), wr_res=(r_scal,), tiny=True)

        def attn_window(s, ic):
            for m in range(NC):
                ps, rps = (pS[2], r_pS[2]) if m % 2 == 0 else (pS[3], r_pS[3])
                for kc in range(NC):
                    op("tensor", lambda e, k=kc, mm=m, p=ps: e.matmul(
                        p, wq_sb[:, k * D + mm * 128:k * D + mm * 128 + 128],
                        xln2T[:, k * N + ic * 512:k * N + (ic + 1) * 512],
                        start=(k == 0), stop=(k == NC - 1)),
                       rd_res=(r_wq, r_xln2T), wr_res=(rps,), mark=(kc == NC - 1))
                op("vector", lambda e, mm=m, p=ps: e.tensor_copy(
                    qf[:, mm * 512:(mm + 1) * 512], p[:]),
                   rd_res=(rps,), wr_res=(r_qf,), mark=True)
            op("vector", lambda e: e.tensor_reduce(
                cols[:, 385:386], qf[:], AX.X, OP.max, apply_absolute_value=True),
               rd_res=(r_qf,), wr_res=(r_scal,), tiny=True)
            op("tensor", lambda e: e.transpose(
                pS[0][0:1, 0:128], cols[:, 385:386], id32[:]),
               rd_res=(r_scal, r_id32), wr_res=(r_pS[0],), mark=True)
            op("vector", lambda e: e.tensor_reduce(
                scal[0:1, 1:2], pS[0][0:1, 0:128], AX.X, OP.max),
               rd_res=(r_pS[0],), wr_res=(r_scal,), tiny=True)
            op("vector", lambda e: e.scalar_tensor_tensor(
                scal[0:1, 2:3], scal[0:1, 1:2], -64.0, scal[0:1, 0:1],
                OP.mult, OP.mult),
               rd_res=(r_scal,), wr_res=(r_scal,), tiny=True)
            op("tensor", lambda e: e.matmul(
                pS[0][:, 0:1], onesf[0:1, :], scal[0:1, 2:3],
                start=True, stop=True),
               rd_res=(r_scal, r_onesf), wr_res=(r_pS[0],), mark=True)
            op("vector", lambda e: e.tensor_copy(
                negc_col[:], pS[0][:, 0:1]),
               rd_res=(r_pS[0],), wr_res=(r_negc,), tiny=True)

            jts = 4 * ic + 4
            for hg in range(4):
                sb_of = [[pS[0], pS[1]], [pS[2], pS[3]]]
                rb_of = [[r_pS[0], r_pS[1]], [r_pS[2], r_pS[3]]]

                def qk_emit(jt, p2):
                    il = max(0, jt * 128 - ic * 512)
                    m = 2 * hg + p2
                    sbank, rbank = sb_of[p2], rb_of[p2]
                    op("tensor", lambda e, j=jt, mm=m, ill=il, sb=sbank: e.matmul(
                        sb[0][:, ill:512], kdup[0:64, j * 128:(j + 1) * 128],
                        qf[0:64, mm * 512 + ill:(mm + 1) * 512],
                        start=True, stop=True, tile_position=(0, 0)),
                       rd_res=(r_kdup[jt // 4], r_qf), wr_res=(rbank[0],), mark=True)
                    op("tensor", lambda e, j=jt, mm=m, ill=il, sb=sbank: e.matmul(
                        sb[1][:, ill:512], kdup[64:128, j * 128:(j + 1) * 128],
                        qf[64:128, mm * 512 + ill:(mm + 1) * 512],
                        start=True, stop=True, tile_position=(64, 0)),
                       rd_res=(r_kdup[jt // 4], r_qf), wr_res=(rbank[1],), mark=True)

                qk_emit(0, 0)
                qk_emit(0, 1)
                ps4r = pS4.ap().rearrange("p (h w) -> p h w", h=4)
                at4r = attn_t[:].rearrange("p (h w) -> p h w", h=4)
                for jt in range(jts):
                    il = max(0, jt * 128 - ic * 512)
                    ia = max(ic * 512 + il, jt * 128) - jt * 128
                    ib = min(256, ic * 512 + 512 - jt * 128)
                    for p2 in range(2):
                        h0 = 2 * p2
                        ps3 = ps4r[:, h0:h0 + 2, il:512]
                        at3 = at4r[:, h0:h0 + 2, il:512]
                        op("scalar", lambda e, d=at3, sr=ps3: e.activation(
                            d, sr, AF.Exp, bias=negc_col[:], scale=1.0),
                           rd_res=(r_pS[h0], r_pS[h0 + 1], r_negc),
                           wr_res=(r_attn[h0], r_attn[h0 + 1]),
                           mark=True)
                        if ia < ib:
                            la = jt * 128 + ia - ic * 512
                            lb = jt * 128 + ib - ic * 512
                            for hs2_ in (h0, h0 + 1):
                                hh2 = 4 * hg + hs2_
                                op("vector", lambda e, h3=hh2, h4=hs2_, a=ia, bb=ib,
                                   l1=la, l2=lb: e.tensor_tensor(
                                       attn_t[:, h4 * 512 + l1:h4 * 512 + l2],
                                       attn_t[:, h4 * 512 + l1:h4 * 512 + l2],
                                       eb[:, h3 * 256 + a:h3 * 256 + bb], OP.mult),
                                   rd_res=(r_eb,), wr_res=(r_attn[hs2_],), mark=True)
                        if jt + 1 < jts:
                            qk_emit(jt + 1, p2)
                        for hs in (h0, h0 + 1):
                            op("tensor", lambda e, j=jt, hs2=hs, ill=il: e.matmul(
                                pav[hs2][:, ill:512], vaug[:, j * 128:(j + 1) * 128],
                                attn_t[:, hs2 * 512 + ill:(hs2 + 1) * 512],
                                start=(j == 0), stop=(j == jts - 1)),
                               rd_res=(r_vaug[jt // 4], r_attn[hs]),
                               wr_res=(r_pav[hs],), mark=True)
                for hs in range(4):
                    hh = 4 * hg + hs
                    m = hh // 2
                    rows = (hh % 2) * 64
                    op("scalar", lambda e, h2=hs: e.activation(
                        rsb[64:128, :], pav[h2][64:128, :], AF.Ln),
                       rd_res=(r_pav[hs],), wr_res=(r_rsb,), mark=True)
                    op("scalar", lambda e, h2=hs: e.activation(
                        pav[h2][64:128, :], rsb[64:128, :], AF.Exp, bias=0.0,
                        scale=ccol[64:128, 1:2]),
                       rd_res=(r_rsb, r_ccol), wr_res=(r_pav[hs], r_rsb), mark=True)
                    op("scalar", lambda e, h2=hs: e.activation(
                        rsb[64:128, :], pav[h2][64:128, :], AF.Square),
                       rd_res=(r_pav[hs],), wr_res=(r_rsb,), mark=True)
                    op("vector", lambda e, h2=hs, mm=m, rw=rows: e.scalar_tensor_tensor(
                        afeat[rw:rw + 64, mm * 512:(mm + 1) * 512],
                        pav[h2][0:64, :], 281474976710656.0, rsb[64:128, :],
                        OP.mult, OP.mult),
                       rd_res=(r_pav[hs], r_rsb), wr_res=(r_afeat,), mark=True)
            wpend = None
            for tl in range(4):
                tau = ic * 4 + tl
                xres = r_x[tau]
                xap = x_tok[:, tau * D:(tau + 1) * D]
                bt = pava if tl % 2 == 0 else pavb
                rbt = [r_pav[0], r_pav[1]] if tl % 2 == 0 else [r_pav[2], r_pav[3]]
                banks = [(bt.ap()[:, 0:512], rbt[0]), (bt.ap()[:, 512:1024], rbt[1])]
                for kc in range(NC):
                    for half in range(2):
                        ps, rps = banks[half]
                        op("tensor", lambda e, k=kc, t=tl, h=half, p=ps: e.matmul(
                            p, afeat[:, k * 512 + t * 128:k * 512 + (t + 1) * 128],
                            wout_sb[:, k * D + h * 512:k * D + (h + 1) * 512],
                            start=(k == 0), stop=False),
                           rd_res=(r_afeat, r_wout), wr_res=(rps,),
                           mark=(kc == NC - 1))
                for half in range(2):
                    ps, rps = banks[half]
                    op("tensor", lambda e, h=half, p=ps: e.matmul(
                        p, onesb[0:1, :], bout_sb[0:1, h * 512:(h + 1) * 512],
                        start=False, stop=True),
                       rd_res=(r_onesb, r_bout), wr_res=(rps,), mark=True)

                def wepi(bt=bt, rbt=rbt, xap=xap, xres=xres):
                    op("vector", lambda e, p=bt, xa=xap: e.tensor_tensor(
                        xa, xa, p.ap(), OP.add),
                       rd_res=(rbt[0], rbt[1]), wr_res=(xres,), mark=True)
                if wpend is not None:
                    wpend()
                wpend = wepi
            wpend()
            wt = [ic * 4 + tl for tl in range(4)]
            ln_stats_batch(wt)
            ffn_run(s, wt, xt_vec=True)
            for tau in wt:
                dma(y_hbm[s * N + tau * 128:s * N + (tau + 1) * 128, :],
                    x_tok[:, tau * D:(tau + 1) * D],
                    rd_res=(r_x[tau],), eng="gpsimd")

        for s in range(SEQ):
            if s > 0:
                for tau in range(NT):
                    dma(x_tok[:, tau * D:(tau + 1) * D],
                        x_hbm[s * N + tau * 128:s * N + (tau + 1) * 128, :],
                        wr_res=(r_x[tau],))
            ln_stats_batch(list(range(NT)))
            ffn_run(s, list(range(NT)), hook=setup_hook if s == 0 else None)
            ln_stats_batch(list(range(NT)), need_nmr=True)
            for ic in range(NWIN):
                for tau in range(ic * 4, ic * 4 + 4):
                    ln2_tile(s, tau)
                kv_chunk(s, ic)
                attn_window(s, ic)

        total_st = b.cnt["st"]
        b.emit("gpsimd", lambda e: e.wait_ge(sems["st"], total_st), ())

        def replay(name):
            def f(eng):
                for waits, bld, inc in b.ops[name]:
                    for (sname, v) in waits:
                        eng.wait_ge(sems[sname], v)
                    inst = bld(eng)
                    if inc is not None:
                        inst.then_inc(sems[inc], 16 if inc in DMA_SEMS else 1)
            return f

        block.sync(replay("sync"))
        block.tensor(replay("tensor"))
        block.vector(replay("vector"))
        block.scalar(replay("scalar"))
        block.gpsimd(replay("gpsimd"))

    return nc


_NC_CACHE = None


def kernel(L0_attn_x, L0_attn_Wq, L0_attn_Wkv, L0_attn_Wout, L0_attn_bout,
           L0_attn_rel_emb, L0_ffn_weight, L0_ffn_bias):
    global _NC_CACHE
    ohd, ident32, identbf = _host_consts()
    if _NC_CACHE is None:
        _NC_CACHE = build_nc()
    nc = _NC_CACHE
    x = np.ascontiguousarray(np.asarray(L0_attn_x, dtype=np.float32))
    base = dict(
        wq=np.ascontiguousarray(np.asarray(L0_attn_Wq, np.float32)),
        wkv=np.ascontiguousarray(np.asarray(L0_attn_Wkv, np.float32)),
        wout=np.ascontiguousarray(np.asarray(L0_attn_Wout, np.float32)),
        bout=np.asarray(L0_attn_bout, np.float32).reshape(1, D),
        remb=np.ascontiguousarray(np.asarray(L0_attn_rel_emb, np.float32)),
        fw=np.ascontiguousarray(np.asarray(L0_ffn_weight, np.float32)),
        fb=np.asarray(L0_ffn_bias, np.float32).reshape(1, D),
        ohd=ohd, id32=ident32, idbf=identbf,
    )
    in_maps = []
    for c in range(N_CORE):
        m = dict(base)
        m["x"] = np.ascontiguousarray(
            x[c * SEQ:(c + 1) * SEQ].reshape(SEQ * N, D))
        in_maps.append(m)
    res = run_bass_kernel_spmd(nc, in_maps, core_ids=list(range(N_CORE)))
    outs = [np.asarray(r["y"], dtype=np.float32).reshape(SEQ, N, D)
            for r in res.results]
    return np.concatenate(outs, axis=0)
